# version 03
# baseline (speedup 1.0000x reference)
import numpy as np
from contextlib import ExitStack
import concourse.bass as bass
import concourse.mybir as mybir
from concourse.bass_utils import run_bass_kernel_spmd

F32 = mybir.dt.float32
BF16 = mybir.dt.bfloat16
AF = mybir.ActivationFunctionType
ALU = mybir.AluOpType
AX = mybir.AxisListType.X

B, T, C, NH, DH, NE, HID = 4, 4096, 1024, 16, 64, 16, 1024
LOCAL = 256
NT = 16
OWN = NT * 128
XR = LOCAL + OWN + 4
NMK = 3 * 128 + 4
NEG = -1.0e9
EPS = 1e-5

DEBUG_TAPS = []
PHASE_LIMIT = 9


def _masks_for(start):
    i = np.arange(T)[:, None]
    j = np.arange(T)[None, :]
    causal = j <= i
    local = causal & (j >= i - (LOCAL - 1))
    out = np.full((NT, 128, NMK), NEG, np.float32)
    for qb in range(NT):
        qi = start + qb * 128 + np.arange(128)
        for slot in range(3):
            kj = start - LOCAL + (qb + slot) * 128 + np.arange(128)
            ok = (kj >= 0) & (kj < T)
            sub = np.zeros((128, 128), bool)
            sub[:, ok] = local[np.ix_(qi, kj[ok])]
            out[qb, :, slot * 128:(slot + 1) * 128] = np.where(sub, 0.0, NEG)
        for g in range(4):
            act = causal[qi, g] & ~local[qi, g]
            out[qb, :, 384 + g] = np.where(act, 0.0, NEG)
    return out


def _host_maps(inputs):
    x = np.asarray(inputs["L0_attn_x"], np.float32)
    w_attn = np.asarray(inputs["L0_attn_w_attn"], np.float32)
    w_proj = np.asarray(inputs["L0_attn_w_proj"], np.float32)
    Wr = np.asarray(inputs["L0_ffn_Wr"], np.float32)
    br = np.asarray(inputs["L0_ffn_br"], np.float32)
    W1 = np.ascontiguousarray(np.asarray(inputs["L0_ffn_W1"], np.float32))
    b1 = np.asarray(inputs["L0_ffn_b1"], np.float32)
    g1 = np.asarray(inputs["L0_ffn_g1"], np.float32)
    be1 = np.asarray(inputs["L0_ffn_be1"], np.float32)
    W2 = np.ascontiguousarray(np.asarray(inputs["L0_ffn_W2"], np.float32))
    b2 = np.asarray(inputs["L0_ffn_b2"], np.float32)
    cc = np.asarray(inputs["L0_c"], np.float32)
    wmod = np.ascontiguousarray(np.asarray(inputs["L0_wmod"], np.float32))

    wattnT = w_attn.T
    wattnT_pp = np.ascontiguousarray(
        wattnT.reshape(8, 128, 3 * C).transpose(1, 0, 2).reshape(128, 8 * 3 * C))
    wprojT = np.ascontiguousarray(w_proj.T)
    g1pp = np.ascontiguousarray(g1.reshape(NE, 8, 128).transpose(2, 0, 1).reshape(128, 128))
    be1pp = np.ascontiguousarray(be1.reshape(NE, 8, 128).transpose(2, 0, 1).reshape(128, 128))
    b1row = np.ascontiguousarray(b1.reshape(1, NE * HID))
    idn = np.eye(128, dtype=np.float32)
    masks = [np.ascontiguousarray(_masks_for(0).reshape(NT * 128, NMK)),
             np.ascontiguousarray(_masks_for(OWN).reshape(NT * 128, NMK))]

    maps = []
    for ci in range(8):
        b, half = ci // 2, ci % 2
        start = half * OWN
        xcore = np.zeros((XR, C), np.float32)
        if start >= LOCAL:
            xcore[0:LOCAL] = x[b, start - LOCAL:start]
        xcore[LOCAL:LOCAL + OWN] = x[b, start:start + OWN]
        xcore[LOCAL + OWN:] = x[b, 0:4]
        maps.append({
            "xc": np.ascontiguousarray(xcore),
            "wattnT_pp": wattnT_pp, "wprojT": wprojT,
            "Wr": np.ascontiguousarray(Wr),
            "br": np.ascontiguousarray(np.broadcast_to(br.reshape(1, NE), (128, NE))),
            "W1": W1, "b1row": b1row, "W2": W2,
            "b2": np.ascontiguousarray(b2),
            "g1pp": g1pp, "be1pp": be1pp,
            "cvec": np.ascontiguousarray(cc[b]), "wmod": wmod,
            "idn": idn, "maskst": masks[half],
        })
    return maps


class _Plan:
    ENGS = ("sync", "scalar", "vector", "tensor", "gpsimd")

    def __init__(self):
        self.q = {k: [] for k in self.ENGS}
        self.cnt = {s: 0 for s in ("sLD", "sGD", "sPE", "sSC", "sVE")}
        self.lastwait = {k: {} for k in self.ENGS}
        self.free = {}

    def op(self, eng, make, waits=(), inc=None, n=1):
        lw = self.lastwait[eng]
        ws = []
        for t in waits:
            if not t:
                continue
            s, v = t
            if v > 0 and lw.get(s, 0) < v:
                lw[s] = v
                ws.append((s, v))
        tgt = None
        if inc is not None:
            self.cnt[inc] += n
            tgt = (inc, n)
        self.q[eng].append((ws, make, tgt))
        return (inc, self.cnt[inc]) if inc else None

    def fre(self, key):
        return self.free.get(key)

    def now(self, sem):
        return (sem, self.cnt[sem])


def _build_nc():
    nc = bass.Bass("TRN2", target_bir_lowering=False)
    dr = lambda n, s, d=F32, k="ExternalInput": nc.dram_tensor(n, s, d, kind=k).ap()

    xc_d = dr("xc", [XR, C])
    watpp_d = dr("wattnT_pp", [128, 8 * 3 * C])
    wprojT_d = dr("wprojT", [C, C])
    Wr_d = dr("Wr", [C, NE])
    br_d = dr("br", [128, NE])
    W1_d = dr("W1", [NE, C, HID])
    b1row_d = dr("b1row", [1, NE * HID])
    W2_d = dr("W2", [NE, HID, C])
    b2_d = dr("b2", [NE, C])
    g1pp_d = dr("g1pp", [128, 128])
    be1pp_d = dr("be1pp", [128, 128])
    cvec_d = dr("cvec", [C])
    wmod_d = dr("wmod", [C, 6 * C])
    idn_d = dr("idn", [128, 128])
    maskst_d = dr("maskst", [NT * 128, NMK])

    y_d = dr("y", [OWN, C], F32, "ExternalOutput")
    x1scr = dr("x1scr", [OWN, C], F32,
               "ExternalOutput" if "x1" in DEBUG_TAPS else "Internal")
    accscr = dr("accscr", [OWN, C], F32,
                "ExternalOutput" if "acc" in DEBUG_TAPS else "Internal")
    taps = {}
    for tname, tshape, tdt in (("mod", [128, 48], F32), ("w", [128, NT * NE], F32),
                               ("st1", [128, 120], F32), ("rt", [128, 80], F32),
                               ("ea", [128, NT * NE], F32),
                               ("lg", [128, NT * NE], F32),
                               ("hT", [128, 8 * XR], BF16),
                               ("h2T", [128, 8 * OWN], BF16),
                               ("yT", [128, 8 * OWN], BF16)):
        if tname in DEBUG_TAPS:
            taps[tname] = dr("tap_" + tname, tshape, tdt, "ExternalOutput")

    P = _Plan()
    LD, GD, PE, SC, VE = "sLD", "sGD", "sPE", "sSC", "sVE"
    op, fre = P.op, P.fre

    es = ExitStack()
    mk = lambda n, p, w, d: es.enter_context(nc.sbuf_tensor(n, [p, w], d))

    id32 = mk("id32", 128, 128, F32)
    id_bf = mk("id_bf", 128, 128, BF16)
    onesf = mk("onesf", 128, 128, F32)
    ones_bf = mk("ones_bf", 128, 128, BF16)
    modpp = mk("modpp", 128, 48, F32)
    ca1p = mk("ca1p", 128, 8, F32)
    cb1p = mk("cb1p", 128, 8, F32)
    gaP = mk("gaP", 128, 8, F32)
    gbP = mk("gbP", 128, 8, F32)
    mtmp = mk("mtmp", 128, 8, F32)
    msk8 = mk("msk8", 128, 8, F32)
    sgn8 = mk("sgn8", 128, 8, F32)
    mx8 = mk("mx8", 128, 8, F32)
    gb_b = mk("gb_b", 128, C, BF16)
    diagb = mk("diagb", 128, 128, BF16)
    siluc = mk("siluc", 128, 8, F32)
    ccpp = mk("ccpp", 128, 8, F32)
    g1pp = mk("g1pp_s", 128, 128, F32)
    be1pp = mk("be1pp_s", 128, 128, F32)
    wrbf = mk("wrbf", 128, 8 * NE, BF16)
    br32 = mk("br32", 128, NE, F32)
    b1stN = mk("b1stN", 1, HID, F32)
    b1e = mk("b1e", 1, 2 * HID, BF16)
    b2bf = mk("b2bf", 16, C, BF16)
    sums = mk("sums", 128, 20, F32)
    msums = mk("msums", 128, 20, F32)
    mub = mk("mub", 128, 20, F32)
    varb = mk("varb", 128, 20, F32)
    stdb = mk("stdb", 128, 20, F32)
    rstdb = mk("rstdb", 128, 20, F32)
    vtmp = mk("vtmp", 128, 20, F32)
    sx = [mk("sx%d" % i, 128, C, F32) for i in range(2)]
    nrm = [mk("nrm%d" % i, 128, C, BF16) for i in range(2)]
    sqsc = mk("sqsc", 128, C, BF16)
    wbuf = mk("wbuf", 128, NT * NE, F32)
    wTb = mk("wTb", 16, OWN, BF16)
    ebuf = mk("ebuf", 128, NE, F32)
    e2buf = mk("e2buf", 128, NE, F32)
    keep = mk("keep", 128, NE, F32)
    inv1 = mk("inv1", 128, NE, F32)
    wrawb = mk("wrawb", 128, NE, F32)
    m1b = mk("m1b", 128, 1, F32)
    m2b = mk("m2b", 128, 1, F32)
    dnb = mk("dnb", 128, 1, F32)
    dninv = mk("dninv", 128, 1, F32)
    wbf_t = mk("wbf_t", 128, NE, BF16)
    rowm = mk("rowm", 128, 1, F32)
    nmb = mk("nmb", 128, 1, F32)
    rsumb = mk("rsumb", 128, 1, F32)
    rinvb = mk("rinvb", 128, 1, F32)
    outb = [mk("outb%d" % i, 128, C, F32) for i in range(2)]
    lgbuf = mk("lgbuf", 128, NT * NE, F32) if "lg" in DEBUG_TAPS else None
    lgsb = mk("lgsb", 128, NE, F32)
    accst = [mk("accst%d" % i, 128, C, F32) for i in range(2)]
    ga_b = accst[0].ap()
    PTg = [mk("PTg%d" % i, 4, 128, BF16) for i in range(2)]
    Vgb = mk("Vgb", 4, 128, BF16)
    aT = [mk("aT%d" % i, 128, 1024, BF16) for i in range(2)]

    bigA = mk("bigA", 128, 8 * XR, BF16)
    bigB = mk("bigB", 128, 8 * OWN, BF16)
    bigC = mk("bigC", 128, 8 * OWN, BF16)
    bigD = mk("bigD", 128, 8192, F32)

    bigA32 = bigA.ap().bitcast(F32)
    b2st = bigA32[:, 0:1024][0:16, :]
    wr32 = bigA32[:, 1024:1152]
    e_all = bigA32[:, 1280:1536]
    kbuf = bigA32[:, 1536:1792]
    invb = bigA32[:, 1792:2048]
    bigC32 = bigC.ap().bitcast(F32)
    bigD16 = bigD.ap().bitcast(BF16)

    hT = [bigA.ap()[:, c * XR:(c + 1) * XR] for c in range(8)]
    W1s = [bigA.ap()[:, s * 8192:(s + 1) * 8192] for s in range(2)]
    wstg = [bigA32[:, 8192 + i * 512:8192 + (i + 1) * 512] for i in range(2)]
    tb5 = bigA32[:, 0:1024]

    yT = [bigB.ap()[:, c * OWN:(c + 1) * OWN] for c in range(8)]
    W2s = [bigB.ap()[:, s * 8192:(s + 1) * 8192] for s in range(2)]

    wmst = [bigC32[:, i * 1536:(i + 1) * 1536] for i in range(2)]
    QT = bigC.ap()[:, 0:2048]
    KT = bigC.ap()[:, 2048:2048 + XR]
    VT = bigC.ap()[:, 4356:4356 + XR]
    Vb = bigC.ap()[:, 6664:6664 + 2304]
    watbf = [bigC.ap()[:, 8968 + i * 3072:8968 + (i + 1) * 3072] for i in range(2)]
    Pb = bigC.ap()[:, 15112:15112 + NMK]
    PTq = [bigC.ap()[:, 15500 + i * 384:15500 + (i + 1) * 384] for i in range(2)]
    h2T = [bigC.ap()[:, c * OWN:(c + 1) * OWN] for c in range(8)]

    maskS = bigD.ap()[:, 0:NT * NMK]
    msb = bigD.ap()[:, 7744:7744 + NMK]
    watst = bigD.ap()[:, 6208:6208 + 1536]
    wpst = bigD.ap()[:, 6208:6208 + 1024]
    wpbf = [bigD16[:, c * 1024:(c + 1) * 1024] for c in range(8)]
    h1stage = bigD16

    pM = [es.enter_context(nc.psum_tensor("pM%d" % i, [128, 1024], F32))
          for i in range(4)]

    vop = lambda mkf, w=(): op("vector", mkf, waits=w, inc=VE)
    spbuf = mk("spbuf", 128, 16, F32)

    def vsp():
        op("vector", lambda h: h.tensor_copy(out=spbuf[:], in_=onesf[:, 0:16]))

    def stok():
        return op("scalar", lambda h: h.copy(out=spbuf[:, 0:16].bitcast(F32)
                  if False else spbuf[:], in_=onesf[:, 0:16]), inc=SC)

    def vtok():
        return op("vector", lambda h: h.tensor_copy(out=spbuf[:],
                                                    in_=onesf[:, 0:16]), inc=VE)

    c_id = op("sync", lambda h: h.dma_start(out=id32[:], in_=idn_d), inc=LD, n=16)
    c_cb = op("sync", lambda h: h.dma_start(
        out=ccpp[:], in_=cvec_d.rearrange("(k p) -> p k", p=128),
        allow_slow_non_contiguous=True), inc=LD, n=16)
    c_msk = op("sync", lambda h: h.dma_start(
        out=maskS.rearrange("p (q m) -> p q m", q=NT),
        in_=maskst_d.rearrange("(q p) m -> p q m", p=128)), inc=LD, n=16)
    c_g1l = op("sync", lambda h: h.dma_start(out=g1pp[:], in_=g1pp_d), inc=LD, n=16)
    c_be1l = op("sync", lambda h: h.dma_start(out=be1pp[:], in_=be1pp_d), inc=LD, n=16)
    c_br = op("sync", lambda h: h.dma_start(out=br32[:], in_=br_d), inc=LD, n=16)
    c_b2 = op("sync", lambda h: h.dma_start(out=b2st, in_=b2_d), inc=LD, n=16)
    c_wrl = op("sync", lambda h: h.dma_start(
        out=wr32.rearrange("p (a e) -> p a e", a=8),
        in_=Wr_d.rearrange("(a p) e -> p a e", p=128)), inc=LD, n=16)
    c_ones = op("vector", lambda h: h.memset(onesf[:], 1.0), inc=VE)
    c_idc = op("scalar", lambda h: h.copy(out=id_bf[:], in_=id32[:]), waits=[c_id], inc=SC)
    c_onesc = op("scalar", lambda h: h.copy(out=ones_bf[:], in_=onesf[:]),
                 waits=[c_ones], inc=SC)
    c_b2c = op("scalar", lambda h: h.copy(out=b2bf[:], in_=b2st), waits=[c_b2], inc=SC)
    c_wrc = op("scalar", lambda h: h.copy(out=wrbf[:], in_=wr32), waits=[c_wrl], inc=SC)
    c_wrc = c_idc = c_onesc = c_brc = c_b2c = stok()

    c_sil = op("scalar", lambda h: h.activation(out=siluc[:], in_=ccpp[:], func=AF.Silu),
               waits=[c_cb], inc=SC)
    mm_last = None
    for cch in range(8):
        for fq in range(4):
            sl = cch * 4 + fq
            buf = wmst[sl % 2]
            cl = op("sync", lambda h, cch=cch, fq=fq, buf=buf: h.dma_start(
                out=buf, in_=wmod_d[cch * 128:(cch + 1) * 128,
                                    fq * 1536:(fq + 1) * 1536]),
                waits=[fre(("wmst", sl % 2))], inc=LD, n=16)
            for jj in range(12):
                j = fq * 12 + jj
                mm_last = op("tensor", lambda h, buf=buf, jj=jj, j=j, cch=cch: h.matmul(
                    out=pM[3][:, j * 8 + cch:j * 8 + cch + 1],
                    lhsT=buf[:, jj * 128:(jj + 1) * 128],
                    rhs=siluc[:, cch:cch + 1], start=True, stop=True),
                    waits=[cl, c_sil], inc=PE)
            P.free[("wmst", sl % 2)] = mm_last
    c_modv = op("vector", lambda h: h.reduce_sum(
        out=modpp[:].rearrange("p (j o) -> p j o", o=1),
        in_=pM[3][:, 0:384].rearrange("p (j k) -> p j k", k=8), axis=AX),
        waits=[mm_last], inc=VE)
    c_modc = op("scalar", lambda h: h.copy(out=modpp[:], in_=modpp[:]) if False
                else h.nop(), waits=[c_modv], inc=SC)
    P.free["pM3"] = c_modc
    if "mod" in DEBUG_TAPS:
        op("sync", lambda h: h.dma_start(out=taps["mod"], in_=modpp[:]),
           waits=[c_modc], inc=LD, n=16)

    vsp()
    vop(lambda h: h.tensor_scalar(out=ca1p[:], in0=modpp[:, 8:16], scalar1=1.0,
                                  scalar2=None, op0=ALU.add), [c_modc])
    vop(lambda h: h.tensor_scalar(out=cb1p[:], in0=modpp[:, 32:40], scalar1=1.0,
                                  scalar2=None, op0=ALU.add))
    c_drv = None
    for gcol, dst in ((2, gaP), (5, gbP)):
        src = modpp[:, gcol * 8:(gcol + 1) * 8]
        cab = op("scalar", lambda h, src=src: h.activation(out=mtmp[:], in_=src,
                 func=AF.Abs), waits=[c_modc], inc=SC)
        vop(lambda h: h.tensor_scalar(out=mx8[:], in0=mtmp[:], scalar1=0.25,
                                      scalar2=None, op0=ALU.max), [cab])
        vop(lambda h, src=src: h.tensor_scalar(out=msk8[:], in0=src, scalar1=0.0,
                                               scalar2=None, op0=ALU.is_ge))
        vsp()
        vop(lambda h: h.tensor_scalar(out=sgn8[:], in0=msk8[:], scalar1=2.0,
                                      scalar2=-1.0, op0=ALU.mult, op1=ALU.add))
        vsp()
        c_drv = vop(lambda h, dst=dst: h.tensor_tensor(out=dst[:], in0=sgn8[:],
                                                       in1=mx8[:], op=ALU.mult))

    for gsel, dst in ((gaP, None), (gbP, gb_b)):
        for k in range(8):
            vsp()
            cdg = vop(lambda h, gsel=gsel, k=k: h.tensor_scalar(
                out=diagb[:], in0=id_bf[:], scalar1=gsel[:, k:k + 1], scalar2=None,
                op0=ALU.mult), [c_drv, c_idc, fre("diagb")])
            cdg = vtok()
            cpe = op("tensor", lambda h: h.matmul(out=pM[2][:, 0:128], lhsT=ones_bf[:],
                     rhs=diagb[:], start=True, stop=True),
                     waits=[cdg, c_onesc, fre("pM2a")], inc=PE)
            P.free["diagb"] = cpe
            c_gab = op("scalar", lambda h, dst=dst, k=k: h.copy(
                out=(ga_b[:, k * 128:(k + 1) * 128] if dst is None
                     else dst[:, k * 128:(k + 1) * 128]), in_=pM[2][:, 0:128]),
                waits=[cpe], inc=SC)
            P.free["pM2a"] = c_gab

    if PHASE_LIMIT < 1:
        return _finish(nc, P, es)
    tiles = [(i * 128, 128, i) for i in range(18)] + [(2304, 4, 18)]
    for (r0, nr, ti) in tiles:
        bb = sx[ti % 2]
        cl = op("sync", lambda h, r0=r0, nr=nr, bb=bb: h.dma_start(
            out=bb[0:nr, :], in_=xc_d[r0:r0 + nr, :]),
            waits=[fre(("sxV", ti % 2)), fre(("sxS", ti % 2))], inc=LD, n=16)
        cs = op("scalar", lambda h, bb=bb, nr=nr, ti=ti: h.activation(
            out=sqsc[0:nr, :], in_=bb[0:nr, :], func=AF.Square,
            accum_out=msums[0:nr, ti:ti + 1]), waits=[cl], inc=SC)
        cv = op("vector", lambda h, bb=bb, nr=nr, ti=ti: h.reduce_sum(
            out=sums[0:nr, ti:ti + 1], in_=bb[0:nr, :], axis=AX), waits=[cl], inc=VE)
        P.free[("sxV", ti % 2)] = cv
        P.free[("sxS", ti % 2)] = cs
    stok()

    def stats(ncols, extra_waits=()):
        vsp()
        c1 = vop(lambda h, ncols=ncols: h.tensor_scalar(
            out=mub[:, 0:ncols], in0=sums[:, 0:ncols], scalar1=1.0 / C,
            scalar2=None, op0=ALU.mult), list(extra_waits))
        c1 = vtok()
        c2 = op("scalar", lambda h, ncols=ncols: h.activation(
            out=vtmp[:, 0:ncols], in_=mub[:, 0:ncols], func=AF.Square),
            waits=[c1], inc=SC)
        c_rd = vop(lambda h, ncols=ncols: h.tensor_scalar(out=varb[:, 0:ncols],
                   in0=msums[:, 0:ncols], scalar1=1.0 / C, scalar2=None, op0=ALU.mult))
        vsp()
        vop(lambda h, ncols=ncols: h.tensor_tensor(out=stdb[:, 0:ncols],
            in0=varb[:, 0:ncols], in1=vtmp[:, 0:ncols], op=ALU.subtract), [c2])
        vsp()
        cve = vop(lambda h, ncols=ncols: h.tensor_scalar(out=varb[:, 0:ncols],
                  in0=stdb[:, 0:ncols], scalar1=EPS, scalar2=None, op0=ALU.add))
        cve = vtok()
        csq = op("scalar", lambda h, ncols=ncols: h.activation(
            out=stdb[:, 0:ncols], in_=varb[:, 0:ncols], func=AF.Sqrt),
            waits=[cve], inc=SC)
        csq = stok()
        crc = op("vector", lambda h, ncols=ncols: h.reciprocal(
            out=rstdb[:, 0:ncols], in_=stdb[:, 0:ncols]), waits=[csq], inc=VE)
        crc = vtok()
        vsp()
        return crc, c_rd

    c_rstdA, _ = stats(19, [P.now(SC), P.now(VE)])

    c_hT = None
    for (r0, nr, ti) in tiles:
        bb = sx[ti % 2]
        nb = nrm[ti % 2]
        cl = op("sync", lambda h, r0=r0, nr=nr, bb=bb: h.dma_start(
            out=bb[0:nr, :], in_=xc_d[r0:r0 + nr, :]),
            waits=[fre(("sxV", ti % 2)), fre(("sxS", ti % 2))], inc=LD, n=16)
        cn = op("vector", lambda h, bb=bb, nb=nb, nr=nr, ti=ti: h.tensor_scalar(
            out=nb[0:nr, :], in0=bb[0:nr, :], scalar1=mub[0:nr, ti:ti + 1],
            scalar2=rstdb[0:nr, ti:ti + 1], op0=ALU.subtract, op1=ALU.mult),
            waits=[cl, c_rstdA, fre(("nrm", ti % 2))], inc=VE)
        P.free[("sxV", ti % 2)] = cn
        cn = vtok()
        P.free[("sxS", ti % 2)] = None
        slab = pM[ti % 2]
        cpe = None
        for cch in range(8):
            cpe = op("tensor", lambda h, nb=nb, nr=nr, cch=cch, slab=slab: h.matmul(
                out=slab[:, cch * 128:cch * 128 + nr],
                lhsT=nb[0:nr, cch * 128:(cch + 1) * 128],
                rhs=id_bf[0:nr, 0:nr], start=True, stop=True),
                waits=([cn, c_idc, fre(("pMA", ti % 2))] if cch == 0 else []), inc=PE)
        P.free[("nrm", ti % 2)] = cpe
        for cch in range(8):
            c_hT = op("scalar", lambda h, cch=cch, r0=r0, nr=nr, slab=slab: h.activation(
                out=hT[cch][:, r0:r0 + nr], in_=slab[:, cch * 128:cch * 128 + nr],
                func=AF.Identity, scale=ca1p[:, cch:cch + 1], bias=modpp[:, cch:cch + 1]),
                waits=([cpe, c_modc] if cch == 0 else []), inc=SC)
        P.free[("pMA", ti % 2)] = c_hT
    c_hT = stok()
    if "hT" in DEBUG_TAPS:
        for cch in range(8):
            op("sync", lambda h, cch=cch: h.dma_start(
                out=taps["hT"][:, cch * XR:(cch + 1) * XR], in_=hT[cch]),
                waits=[c_hT], inc=LD, n=16)

    if "st1" in DEBUG_TAPS:
        def dump_st(h):
            return h.dma_start(out=taps["st1"].rearrange("p (a w) -> p a w", a=6),
                               in_=None)
        for si, buf in enumerate((sums, msums, mub, varb, stdb, rstdb)):
            op("sync", lambda h, si=si, buf=buf: h.dma_start(
                out=taps["st1"][:, si * 20:(si + 1) * 20], in_=buf[:]),
                waits=[c_rstdA, P.now(SC), P.now(VE)], inc=LD, n=16)
    if PHASE_LIMIT < 2:
        return _finish(nc, P, es)
    qkv_kv = [(0, 512), (512, 512), (1024, 512), (1536, 512), (2048, 260)]
    qkv_q = [(256, 512), (768, 512), (1280, 512), (1792, 512)]
    c_yT = None
    c_ms = None
    for hg in range(8):
        wb = watbf[hg % 2]
        pieces = [(pi, a) for pi in range(3) for a in range(8)]
        lds_tok = None
        for half2 in range(2):
            ldw = None
            for k in range(12):
                p = half2 * 12 + k
                pi, a = pieces[p]
                fbase = pi * C + hg * 128
                ldw = op("sync", lambda h, a=a, fbase=fbase, k=k: h.dma_start(
                    out=watst[:, k * 128:(k + 1) * 128],
                    in_=watpp_d[:, a * 3072 + fbase:a * 3072 + fbase + 128]),
                    waits=[fre("watst"),
                           (fre(("watbf", hg % 2)) if (half2 == 0 and k == 0)
                            else None)] if k == 0 else [], inc=LD, n=16)
            lds_tok = op("scalar", lambda h, wb=wb, half2=half2: h.copy(
                out=wb[:, half2 * 1536:(half2 + 1) * 1536], in_=watst),
                waits=[ldw], inc=SC)
            P.free["watst"] = lds_tok
            lds_tok = stok()
        pp = 0
        c_QT = c_KT = c_VT = None
        for pi, (dstT, ngl) in enumerate(((QT, qkv_q), (KT, qkv_kv), (VT, qkv_kv))):
            for (col0, ncols) in ngl:
                half = pp % 2
                base = half * 512
                dcol0 = col0 - 256 if pi == 0 else col0
                last = None
                for a in range(8):
                    last = op("tensor", lambda h, a=a, pi=pi, col0=col0, ncols=ncols,
                              wb=wb, base=base: h.matmul(
                        out=pM[0][:, base:base + ncols],
                        lhsT=wb[:, (pi * 8 + a) * 128:(pi * 8 + a) * 128 + 128],
                        rhs=hT[a][:, col0:col0 + ncols],
                        start=(a == 0), stop=(a == 7)),
                        waits=([lds_tok, c_hT, fre(("pMqkv", half))]
                               if a == 0 else []), inc=PE)
                cq = op("scalar", lambda h, dstT=dstT, dcol0=dcol0, ncols=ncols,
                        base=base, pi=pi: (
                    h.mul(out=dstT[:, dcol0:dcol0 + ncols],
                          in_=pM[0][:, base:base + ncols], mul=0.125) if pi == 0
                    else h.copy(out=dstT[:, dcol0:dcol0 + ncols],
                                in_=pM[0][:, base:base + ncols])),
                        waits=[last], inc=SC)
                P.free[("pMqkv", half)] = cq
                pp += 1
            if pi == 0:
                c_QT = stok()
            elif pi == 1:
                c_KT = stok()
            else:
                c_VT = stok()
        P.free[("watbf", hg % 2)] = last
        c_vt = None
        vt_jobs = [(g, hp, False) for g in range(18) for hp in range(2)]
        vt_jobs += [(None, hp, True) for hp in range(2)]
        for i, (g, hp, isglob) in enumerate(vt_jobs):
            bank = i % 2
            base = bank * 512
            if not isglob:
                cpe = op("tensor", lambda h, g=g, hp=hp, base=base: h.matmul(
                    out=pM[1][:, base:base + 64],
                    lhsT=VT[64 * hp:64 * hp + 64, g * 128:(g + 1) * 128],
                    rhs=id_bf[64 * hp:64 * hp + 64, 64 * hp:64 * hp + 64],
                    start=True, stop=True),
                    waits=[c_VT, fre(("pMvt", bank))], inc=PE)
                c_vt = op("scalar", lambda h, g=g, hp=hp, base=base: h.copy(
                    out=Vb[:, g * 128 + 64 * hp:g * 128 + 64 * hp + 64],
                    in_=pM[1][:, base:base + 64]),
                    waits=[cpe], inc=SC)
            else:
                cpe = op("tensor", lambda h, hp=hp, base=base: h.matmul(
                    out=pM[1][0:4, base:base + 64],
                    lhsT=VT[64 * hp:64 * hp + 64, 2304:2308],
                    rhs=id_bf[64 * hp:64 * hp + 64, 64 * hp:64 * hp + 64],
                    start=True, stop=True),
                    waits=[c_VT, fre(("pMvt", bank))], inc=PE)
                c_vt = op("scalar", lambda h, hp=hp, base=base: h.copy(
                    out=Vgb[0:4, 64 * hp:64 * hp + 64],
                    in_=pM[1][0:4, base:base + 64]),
                    waits=[cpe], inc=SC)
            P.free[("pMvt", bank)] = c_vt
        c_vt = stok()
        P.free[("pMT", 0)] = c_vt
        P.free[("pMT", 1)] = c_vt
        P.free[("pMvt", 0)] = None
        P.free[("pMvt", 1)] = None

        cc2 = None
        for hp in range(2):
            for qb in range(NT):
                sS = qb % 2
                sbase = sS * 512
                sT = qb % 2
                tbase = sT * 512
                qg = qb // 4
                ybase = (qg % 2) * 512
                csa = op("tensor", lambda h, hp=hp, qb=qb, sbase=sbase: h.matmul(
                    out=pM[2][:, sbase:sbase + 384],
                    lhsT=QT[64 * hp:64 * hp + 64, qb * 128:qb * 128 + 128],
                    rhs=KT[64 * hp:64 * hp + 64, qb * 128:qb * 128 + 384],
                    start=True, stop=True),
                    waits=[c_QT, c_KT, fre(("pMS", sS)), fre("pM2a")], inc=PE)
                csb = op("tensor", lambda h, hp=hp, qb=qb, sbase=sbase: h.matmul(
                    out=pM[2][:, sbase + 384:sbase + 388],
                    lhsT=QT[64 * hp:64 * hp + 64, qb * 128:qb * 128 + 128],
                    rhs=KT[64 * hp:64 * hp + 64, 2304:2308],
                    start=True, stop=True), inc=PE)
                c_ms = op("vector", lambda h, qb=qb, sbase=sbase: h.tensor_tensor(
                    out=msb, in0=pM[2][:, sbase:sbase + 388],
                    in1=maskS[:, qb * NMK:qb * NMK + 388], op=ALU.add),
                    waits=[csb, c_msk, fre("msb")], inc=VE)
                P.free[("pMS", sS)] = c_ms
                c_nm = op("vector", lambda h: h.reduce_max(
                    out=rowm[:], in_=msb, axis=AX, negate=True), inc=VE)
                c_nm = vtok()
                c_exp = op("scalar", lambda h: h.activation(
                    out=Pb[:], in_=msb, func=AF.Exp, scale=1.0, bias=rowm[:],
                    accum_out=rsumb[:]), waits=[c_nm, fre("Pb")], inc=SC)
                P.free["msb"] = c_exp
                c_exp = stok()
                c_ri = op("vector", lambda h: h.reciprocal(out=rinvb[:], in_=rsumb[:]),
                          waits=[c_exp], inc=VE)
                vsp()
                vsp()
                c_dg = vop(lambda h: h.tensor_scalar(out=diagb[:], in0=id_bf[:],
                           scalar1=rinvb[:], scalar2=None, op0=ALU.mult),
                           [c_ri, fre("diagb")])
                c_dg = vtok()
                cpt = None
                for kbi in range(3):
                    cpt = op("tensor", lambda h, kbi=kbi, tbase=tbase: h.matmul(
                        out=pM[1][:, tbase + kbi * 128:tbase + kbi * 128 + 128],
                        lhsT=Pb[:, kbi * 128:kbi * 128 + 128], rhs=diagb[:],
                        start=True, stop=True),
                        waits=([c_exp, c_dg, fre(("pMT", sT))] if kbi == 0 else []),
                        inc=PE)
                cptg = op("tensor", lambda h, tbase=tbase: h.matmul(
                    out=pM[1][0:4, tbase + 384:tbase + 384 + 128],
                    lhsT=Pb[:, 384:388], rhs=diagb[:], start=True, stop=True), inc=PE)
                P.free["Pb"] = cptg
                P.free["diagb"] = cptg
                cc1 = op("scalar", lambda h, sT=sT, tbase=tbase: h.copy(
                    out=PTq[sT], in_=pM[1][:, tbase:tbase + 384]),
                    waits=[cptg, fre(("PTq", sT))], inc=SC)
                cc2 = op("scalar", lambda h, sT=sT, tbase=tbase: h.copy(
                    out=PTg[sT][:], in_=pM[1][0:4, tbase + 384:tbase + 512]), inc=SC)
                P.free[("pMT", sT)] = cc2
                cc2 = stok()
                cpv = None
                for kbi in range(3):
                    cpv = op("tensor", lambda h, qb=qb, kbi=kbi, hp=hp, ybase=ybase,
                             sT=sT: h.matmul(
                        out=pM[3][0:64, ybase + (qb % 4) * 128:
                                  ybase + (qb % 4) * 128 + 128],
                        lhsT=Vb[:, (qb + kbi) * 128 + 64 * hp:
                                (qb + kbi) * 128 + 64 * hp + 64],
                        rhs=PTq[sT][:, kbi * 128:kbi * 128 + 128],
                        start=(kbi == 0), stop=False),
                        waits=([cc2, c_vt, fre("pM3"),
                                (fre(("pMyT", qg % 2)) if qb % 4 == 0 else None)]
                               if kbi == 0 else []), inc=PE)
                cpv = op("tensor", lambda h, qb=qb, hp=hp, ybase=ybase, sT=sT: h.matmul(
                    out=pM[3][0:64, ybase + (qb % 4) * 128:ybase + (qb % 4) * 128 + 128],
                    lhsT=Vgb[0:4, 64 * hp:64 * hp + 64],
                    rhs=PTg[sT][:], start=False, stop=True), inc=PE)
                P.free[("PTq", sT)] = cpv
                if qb % 4 == 3:
                    c_yT = op("scalar", lambda h, hg=hg, hp=hp, qg=qg, ybase=ybase:
                              h.copy(out=yT[hg][64 * hp:64 * hp + 64,
                                               qg * 512:qg * 512 + 512],
                                     in_=pM[3][0:64, ybase:ybase + 512]),
                              waits=[cpv], inc=SC)
                    P.free[("pMyT", qg % 2)] = c_yT
        for s in range(6):
            P.free[("pMvt", s)] = cc2
    P.free["pMrt"] = c_ms
    if "yT" in DEBUG_TAPS:
        for cch in range(8):
            op("sync", lambda h, cch=cch: h.dma_start(
                out=taps["yT"][:, cch * OWN:(cch + 1) * OWN], in_=yT[cch]),
                waits=[c_yT], inc=LD, n=16)

    b_end_ve = P.now(VE)
    b_end_sc = P.now(SC)
    c_wp = None
    for cch in range(8):
        cl = op("sync", lambda h, cch=cch: h.dma_start(
            out=wpst, in_=wprojT_d[cch * 128:(cch + 1) * 128, :]),
            waits=[b_end_ve, b_end_sc, fre("wpst")], inc=LD, n=16)
        c_wp = vop(lambda h, cch=cch: h.tensor_tensor(
            out=wpbf[cch], in0=wpst, in1=ga_b, op=ALU.mult), [cl, c_gab])
        P.free["wpst"] = c_wp
        c_wp = vtok()

    if PHASE_LIMIT < 3:
        return _finish(nc, P, es)
    c_sp = {}
    for t in range(NT):
        slab = pM[t % 2]
        last = None
        for cch in range(8):
            for nh in range(2):
                last = op("tensor", lambda h, cch=cch, nh=nh, t=t, slab=slab: h.matmul(
                    out=slab[:, nh * 512:nh * 512 + 512],
                    lhsT=yT[cch][:, t * 128:(t + 1) * 128],
                    rhs=wpbf[cch][:, nh * 512:nh * 512 + 512],
                    start=(cch == 0), stop=(cch == 7)),
                    waits=([c_yT, c_wp, fre(("pMC", t % 2))]
                           if (cch == 0 and nh == 0) else []), inc=PE)
        cl = op("sync", lambda h, t=t: h.dma_start(
            out=sx[t % 2][:], in_=xc_d[LOCAL + t * 128:LOCAL + (t + 1) * 128, :]),
            waits=[fre(("sxV", t % 2)), fre(("sxS", t % 2))], inc=LD, n=16)
        cx1 = op("vector", lambda h, t=t, slab=slab: h.tensor_tensor(
            out=outb[t % 2][:], in0=slab[:, 0:1024], in1=sx[t % 2][:], op=ALU.add),
            waits=[last, cl, fre(("outb", t % 2)), fre(("outbS", t % 2)),
                   fre(("outbV", t % 2))], inc=VE)
        P.free[("pMC", t % 2)] = cx1
        P.free[("sxV", t % 2)] = cx1
        P.free[("sxS", t % 2)] = None
        cx1 = vtok()
        cs = op("scalar", lambda h, t=t: h.activation(
            out=sqsc[:], in_=outb[t % 2][:], func=AF.Square,
            accum_out=msums[:, t:t + 1]), waits=[cx1], inc=SC)
        cv = op("vector", lambda h, t=t: h.reduce_sum(
            out=sums[:, t:t + 1], in_=outb[t % 2][:], axis=AX), inc=VE)
        c_sp[t] = op("gpsimd", lambda h, t=t: h.dma_start(
            out=x1scr[t * 128:(t + 1) * 128, :], in_=outb[t % 2][:]),
            waits=[cx1], inc=GD, n=16)
        P.free[("outb", t % 2)] = c_sp[t]
        P.free[("outbS", t % 2)] = cs
        P.free[("outbV", t % 2)] = cv

    stok()
    c_rstdC, _ = stats(16, [P.now(SC), P.now(VE)])

    c_h2T = None
    c_C4 = None
    for t in range(NT):
        cl = op("gpsimd", lambda h, t=t: h.dma_start(
            out=sx[t % 2][:], in_=x1scr[t * 128:(t + 1) * 128, :]),
            waits=[fre(("sxV", t % 2))], inc=GD, n=16)
        cn = op("vector", lambda h, t=t: h.tensor_scalar(
            out=nrm[t % 2][:], in0=sx[t % 2][:], scalar1=mub[:, t:t + 1],
            scalar2=rstdb[:, t:t + 1], op0=ALU.subtract, op1=ALU.mult),
            waits=[cl, c_rstdC, fre(("nrm", t % 2))], inc=VE)
        P.free[("sxV", t % 2)] = cn
        cn = vtok()
        slab = pM[t % 2]
        cpe = None
        for cch in range(8):
            cpe = op("tensor", lambda h, t=t, cch=cch, slab=slab: h.matmul(
                out=slab[:, cch * 128:(cch + 1) * 128],
                lhsT=nrm[t % 2][:, cch * 128:(cch + 1) * 128],
                rhs=id_bf[:], start=True, stop=True),
                waits=([cn, fre(("pMC", t % 2))] if cch == 0 else []), inc=PE)
        P.free[("nrm", t % 2)] = cpe
        for cch in range(8):
            c_h2T = op("scalar", lambda h, t=t, cch=cch, slab=slab: h.activation(
                out=h2T[cch][:, t * 128:(t + 1) * 128],
                in_=slab[:, cch * 128:(cch + 1) * 128],
                func=AF.Identity, scale=cb1p[:, cch:cch + 1],
                bias=modpp[:, 24 + cch:25 + cch]),
                waits=([cpe] if cch == 0 else []), inc=SC)
        P.free[("pMC", t % 2)] = c_h2T
        c_h2T = stok()
        crt = None
        for cch in range(8):
            crt = op("tensor", lambda h, t=t, cch=cch: h.matmul(
                out=pM[2][:, 0:16], lhsT=h2T[cch][:, t * 128:(t + 1) * 128],
                rhs=wrbf[:, cch * 16:(cch + 1) * 16],
                start=(cch == 0), stop=(cch == 7)),
                waits=([c_h2T, c_wrc, fre("pMrt")] if cch == 0 else []), inc=PE)

        clg = op("vector", lambda h: h.tensor_tensor(
            out=lgsb[:], in0=pM[2][:, 0:16], in1=br32[:], op=ALU.add),
            waits=[crt, c_br, fre("lgsb")], inc=VE)
        P.free["pMrt"] = clg
        if "lg" in DEBUG_TAPS:
            op("vector", lambda h, t=t: h.tensor_copy(
                out=lgbuf[:, t * 16:(t + 1) * 16], in_=lgsb[:]), inc=VE)
        cvm = op("vector", lambda h: h.reduce_max(out=rowm[:], in_=lgsb[:],
                 axis=AX, negate=True), inc=VE)
        cvm = vtok()
        cex = op("scalar", lambda h, t=t: h.activation(
            out=e_all[:, t * 16:(t + 1) * 16], in_=lgsb[:],
            func=AF.Exp, scale=1.0, bias=rowm[:]), waits=[cvm], inc=SC)
        P.free["lgsb"] = cex
        c_exp_all = cex

    if "ea" in DEBUG_TAPS:
        op("sync", lambda h: h.dma_start(out=taps["ea"], in_=e_all),
           waits=[P.now(SC), P.now(VE)], inc=LD, n=16)
    if "lg" in DEBUG_TAPS:
        op("sync", lambda h: h.dma_start(out=taps["lg"], in_=lgbuf[:]),
           waits=[P.now(SC), P.now(VE)], inc=LD, n=16)
    c_exp_all = stok()
    e3 = e_all.rearrange("p (t e) -> p t e", e=16)
    m1a, m2a, dna, dnia = ebuf, e2buf, keep, inv1
    m1b3 = m1a.ap().rearrange("p (t o) -> p t o", o=1).broadcast_to([128, 16, 16])
    m2b3 = m2a.ap().rearrange("p (t o) -> p t o", o=1).broadcast_to([128, 16, 16])
    dni3 = dnia.ap().rearrange("p (t o) -> p t o", o=1).broadcast_to([128, 16, 16])
    k3 = kbuf.rearrange("p (t e) -> p t e", e=16)
    i3 = invb.rearrange("p (t e) -> p t e", e=16)
    w3 = wbuf.ap().rearrange("p (t e) -> p t e", e=16)
    vop(lambda h: h.tensor_reduce(out=m1a[:].rearrange("p (t o) -> p t o", o=1),
        in_=e3, axis=AX, op=ALU.max), [c_exp_all])
    vsp()
    vop(lambda h: h.tensor_tensor(out=k3, in0=e3, in1=m1b3, op=ALU.is_ge))
    vop(lambda h: h.tensor_scalar(out=invb, in0=kbuf, scalar1=-1.0, scalar2=1.0,
        op0=ALU.mult, op1=ALU.add))
    vop(lambda h: h.tensor_tensor(out=k3, in0=e3, in1=i3, op=ALU.mult))
    vop(lambda h: h.tensor_reduce(out=m2a[:].rearrange("p (t o) -> p t o", o=1),
        in_=k3, axis=AX, op=ALU.max))
    vsp()
    vop(lambda h: h.tensor_tensor(out=i3, in0=e3, in1=m2b3, op=ALU.is_ge))
    vop(lambda h: h.tensor_tensor(out=k3, in0=e3, in1=i3, op=ALU.mult))
    vop(lambda h: h.tensor_tensor(out=dna[:], in0=m1a[:], in1=m2a[:], op=ALU.add))
    vsp()
    vop(lambda h: h.reciprocal(out=dnia[:], in_=dna[:]))
    vsp()
    cw = vop(lambda h: h.tensor_tensor(out=w3, in0=k3, in1=dni3, op=ALU.mult))
    cw = vtok()

    for t in range(NT):
        cwb = op("scalar", lambda h, t=t: h.copy(out=wbf_t[:],
                 in_=wbuf[:, t * 16:(t + 1) * 16]), waits=[cw, fre("wbf_t")], inc=SC)
        cwb = stok()
        cwt = op("tensor", lambda h: h.matmul(out=pM[2][0:16, 512:640],
                 lhsT=wbf_t[:], rhs=id_bf[:], start=True, stop=True),
                 waits=[cwb, fre("pMwt")], inc=PE)
        P.free["wbf_t"] = cwt
        c_C4 = op("scalar", lambda h, t=t: h.copy(out=wTb[0:16, t * 128:(t + 1) * 128],
                  in_=pM[2][0:16, 512:640]), waits=[cwt], inc=SC)
        P.free["pMwt"] = c_C4
    c_C4 = stok()
    if "rt" in DEBUG_TAPS:
        for si, (buf, wd) in enumerate(((ebuf, 16), (e2buf, 16), (wrawb, 16),
                                        (keep, 16), (m1b, 1), (m2b, 1), (dnb, 1),
                                        (dninv, 1), (rowm, 1))):
            op("sync", lambda h, si=si, buf=buf, wd=wd: h.dma_start(
                out=taps["rt"][:, si * 8 if wd == 1 else si * 16:
                               (si * 8 if wd == 1 else si * 16) + wd][:, 0:wd]
                if False else taps["rt"][:, si * 16:si * 16 + wd] if wd == 16
                else taps["rt"][:, 64 + (si - 4):64 + (si - 4) + 1],
                in_=buf[:, 0:wd], allow_slow_non_contiguous=True),
                waits=[c_C4, P.now(VE), P.now(SC)], inc=LD, n=16)
    if "h2T" in DEBUG_TAPS:
        for cch in range(8):
            op("sync", lambda h, cch=cch: h.dma_start(
                out=taps["h2T"][:, cch * OWN:(cch + 1) * OWN], in_=h2T[cch]),
                waits=[c_h2T], inc=LD, n=16)
    if "w" in DEBUG_TAPS:
        op("sync", lambda h: h.dma_start(out=taps["w"], in_=wbuf[:]),
           waits=[c_C4], inc=LD, n=16)

    if PHASE_LIMIT < 4:
        return _finish(nc, P, es)
    c1_end_pe = P.now(PE)
    wcast_tok = {}
    b1_tok = {}
    NTH = NT // 2
    h1stage = bigD16[:, 0:NTH * 1024]
    slabA = bigD16[:, NTH * 1024:2 * NTH * 1024]
    slabB = mk("slabB", 128, NTH * 1024, BF16)
    slabs = [slabA, slabB]

    def load_cast(which, src_d, e, slot):
        dst = (W1s if which == 1 else W2s)[slot]
        if which == 1:
            clb = op("sync", lambda h, e=e: h.dma_start(
                out=b1stN[:], in_=b1row_d[0:1, e * HID:(e + 1) * HID]),
                waits=[fre("b1stN"), fre(("b1e", e % 2))], inc=LD, n=16)
            b1_tok[e] = op("scalar", lambda h, e=e: h.copy(
                out=b1e[0:1, (e % 2) * HID:(e % 2 + 1) * HID], in_=b1stN[:]),
                waits=[clb], inc=SC)
            P.free["b1stN"] = b1_tok[e]
            b1_tok[e] = stok()
        ctk = None
        for cch in range(8):
            for half2 in range(2):
                si = (cch * 2 + half2) % 2
                stg = wstg[si]
                cl = op("sync", lambda h, e=e, cch=cch, half2=half2, stg=stg,
                        src_d=src_d: h.dma_start(
                    out=stg, in_=src_d[e, cch * 128:(cch + 1) * 128,
                                       half2 * 512:half2 * 512 + 512]),
                    waits=[fre(("wstg", si)), c1_end_pe], inc=LD, n=16)
                ctk = op("vector", lambda h, cch=cch, half2=half2, stg=stg, dst=dst:
                         h.tensor_copy(
                    out=dst[:, cch * 1024 + half2 * 512:cch * 1024 + half2 * 512 + 512],
                    in_=stg),
                    waits=[cl, c1_end_pe] +
                          ([fre(("Wslot", which, slot))] if cch == 0 and half2 == 0
                           else []), inc=VE)
                P.free[("wstg", si)] = ctk
        ctk = vtok()
        wcast_tok[(which, e)] = ctk
        return ctk

    c_stage = {}
    c_sq = {}
    c_h1 = {}

    def emit_h1(e, t, extra=()):
        slot = e % 2
        last = None
        for cch in range(8):
            for nh in range(2):
                w0 = (list(extra) + [c_h2T, wcast_tok[(1, e)], fre("pM0")]
                      if (cch == 0 and nh == 0) else [])
                last = op("tensor", lambda h, e=e, t=t, cch=cch, nh=nh, slot=slot:
                          h.matmul(
                    out=pM[0][:, nh * 512:nh * 512 + 512],
                    lhsT=h2T[cch][:, t * 128:(t + 1) * 128],
                    rhs=W1s[slot][:, cch * 1024 + nh * 512:cch * 1024 + nh * 512 + 512],
                    start=(cch == 0), stop=False), waits=w0, inc=PE)
        for nh in range(2):
            last = op("tensor", lambda h, e=e, nh=nh: h.matmul(
                out=pM[0][:, nh * 512:nh * 512 + 512], lhsT=ones_bf[0:1, :],
                rhs=b1e[0:1, (e % 2) * 1024 + nh * 512:(e % 2) * 1024 + nh * 512 + 512],
                start=False, stop=True),
                waits=([b1_tok[e]] if nh == 0 else []), inc=PE)
        c_h1[(e, t)] = last
        P.free[("b1e", e % 2)] = last
        return last

    def emit_stage(e, t, tl, waits=()):
        cs1 = op("scalar", lambda h, tl=tl: h.activation(
            out=h1stage[:, tl * 1024:(tl + 1) * 1024], in_=pM[0][:, 0:1024],
            func=AF.Copy, accum_out=sums[:, tl:tl + 1]),
            waits=[c_h1[(e, t)]] + list(waits), inc=SC)
        cs2 = op("scalar", lambda h: h.activation(
            out=sqsc[:], in_=pM[0][:, 0:1024], func=AF.Square,
            accum_out=msums[:, tl:tl + 1]), inc=SC)
        c_stage[(e, t)] = cs1
        cs2 = stok()
        c_sq[(e, t)] = cs2
        P.free["pM0"] = cs2
        return cs2

    load_cast(1, W1_d, 0, 0)
    load_cast(2, W2_d, 0, 0)
    load_cast(1, W1_d, 1, 1)

    c_statrd = None
    for HF in range(2):
        TH = [HF * NTH + i for i in range(NTH)]
        for tl, t in enumerate(TH):
            slab = pM[2 + (tl % 2)]
            last = None
            for nh in range(2):
                last = op("tensor", lambda h, t=t, nh=nh, slab=slab: h.matmul(
                    out=slab[:, nh * 512:nh * 512 + 512],
                    lhsT=wTb[0:16, t * 128:(t + 1) * 128],
                    rhs=b2bf[0:16, nh * 512:nh * 512 + 512], start=True, stop=True),
                    waits=([c_C4, c_b2c, fre(("pMW2", tl % 2)), fre("pM3"),
                            fre(("pMyT", 0)), fre(("pMyT", 1))] if nh == 0 else []),
                    inc=PE)
            cvc = op("vector", lambda h, tl=tl, slab=slab: h.tensor_copy(
                out=slabA[:, tl * 1024:(tl + 1) * 1024], in_=slab[:, 0:1024]),
                waits=[last, fre(("slabw", 0, tl))], inc=VE)
            P.free[("pMW2", tl % 2)] = cvc

        statw = [P.now(SC), P.now(VE), c_statrd]
        for tl, t in enumerate(TH):
            emit_h1(0, t, extra=statw + [fre(("pMC", t % 2))])
            emit_stage(0, t, tl, waits=statw)
        P.free[("W1use", 0)] = c_h1[(0, TH[-1])]

        c_norm = {}
        for le in range(NE):
            c_rstdD, c_statrd = stats(NTH, [c_sq[(le, TH[-1])], fre("normlast")])
            if HF == 0 or le + 1 < NE:
                nxt = []
                if le + 1 < NE:
                    nxt.append((2, le + 1, (le + 1) % 2))
                    P.free[("Wslot", 2, (le + 1) % 2)] = fre(("W2use", (le + 1) % 2))
                    if le + 2 < NE:
                        nxt.append((1, le + 2, le % 2))
                        P.free[("Wslot", 1, le % 2)] = fre(("W1use", le % 2))
                else:
                    nxt = [(1, 0, 0), (2, 0, 0), (1, 1, 1)]
                    P.free[("Wslot", 1, 0)] = fre(("W1use", 0))
                    P.free[("Wslot", 2, 0)] = fre(("W2use", 0))
                    P.free[("Wslot", 1, 1)] = fre(("W1use", 1))
                for which, e, slot in nxt:
                    load_cast(which, W1_d if which == 1 else W2_d, e, slot)

            src = slabs[le % 2]
            dst = slabs[(le + 1) % 2]
            if le + 1 < NE:
                emit_h1(le + 1, TH[0])
            for tl, t in enumerate(TH):
                cn = op("vector", lambda h, tl=tl: h.tensor_scalar(
                    out=nrm[tl % 2][:], in0=h1stage[:, tl * 1024:(tl + 1) * 1024],
                    scalar1=mub[:, tl:tl + 1], scalar2=rstdb[:, tl:tl + 1],
                    op0=ALU.subtract, op1=ALU.mult),
                    waits=[c_rstdD, fre(("nrm", tl % 2)), c_stage[(le, t)]], inc=VE)
                c_norm[(le, t)] = cn
                cn = vtok()
                cpe = None
                for cch in range(8):
                    cpe = op("tensor", lambda h, tl=tl, cch=cch: h.matmul(
                        out=pM[1][:, cch * 128:(cch + 1) * 128],
                        lhsT=nrm[tl % 2][:, cch * 128:(cch + 1) * 128],
                        rhs=id_bf[:], start=True, stop=True),
                        waits=([cn, fre("pM1")] if cch == 0 else []), inc=PE)
                P.free[("nrm", tl % 2)] = cpe
                cge = None
                for cch in range(8):
                    cge = op("scalar", lambda h, le=le, tl=tl, cch=cch: h.activation(
                        out=aT[tl % 2][:, cch * 128:(cch + 1) * 128],
                        in_=pM[1][:, cch * 128:(cch + 1) * 128], func=AF.Gelu,
                        scale=g1pp[:, le * 8 + cch:le * 8 + cch + 1],
                        bias=be1pp[:, le * 8 + cch:le * 8 + cch + 1]),
                        waits=([cpe, c_g1l, c_be1l, fre(("aT", tl % 2))]
                               if cch == 0 else []), inc=SC)
                P.free["pM1"] = cge
                cge = stok()
                if le + 1 < NE:
                    emit_stage(le + 1, t, tl, waits=[c_norm[(le, t)], c_statrd])
                if le + 1 < NE and tl + 1 < NTH:
                    emit_h1(le + 1, TH[tl + 1])
                slab = pM[2 + (tl % 2)]
                last = None
                for cch in range(8):
                    for nh in range(2):
                        last = op("tensor", lambda h, le=le, tl=tl, cch=cch, nh=nh,
                                  slab=slab: h.matmul(
                            out=slab[:, nh * 512:nh * 512 + 512],
                            lhsT=aT[tl % 2][:, cch * 128:(cch + 1) * 128],
                            rhs=W2s[le % 2][:, cch * 1024 + nh * 512:
                                             cch * 1024 + nh * 512 + 512],
                            start=(cch == 0), stop=(cch == 7)),
                            waits=([cge, wcast_tok[(2, le)], fre(("pMW2", tl % 2))]
                                   if (cch == 0 and nh == 0) else []), inc=PE)
                P.free[("aT", tl % 2)] = last
                P.free[("W2use", le % 2)] = last
                if le + 1 < NE:
                    P.free[("W1use", (le + 1) % 2)] = c_h1[(le + 1, t)]
                csc = op("vector", lambda h, le=le, t=t, tl=tl, slab=slab, src=src,
                         dst=dst: h.scalar_tensor_tensor(
                    out=dst[:, tl * 1024:(tl + 1) * 1024], in0=slab[:, 0:1024],
                    scalar=wbuf[:, t * 16 + le:t * 16 + le + 1],
                    in1=src[:, tl * 1024:(tl + 1) * 1024],
                    op0=ALU.mult, op1=ALU.add),
                    waits=[last, fre(("slabw", (le + 1) % 2, tl))], inc=VE)
                P.free[("pMW2", tl % 2)] = csc
                P.free[("slabw", le % 2, tl)] = csc
            P.free["normlast"] = c_norm[(le, TH[-1])]

        for tl, t in enumerate(TH):
            cxl = op("sync", lambda h, t=t, tl=tl: h.dma_start(
                out=sx[tl % 2][:], in_=x1scr[t * 128:(t + 1) * 128, :]),
                waits=[fre(("sxV", tl % 2))], inc=LD, n=16)
            cv1 = op("vector", lambda h, tl=tl: h.tensor_tensor(
                out=accst[tl % 2][:], in0=slabA[:, tl * 1024:(tl + 1) * 1024],
                in1=gb_b[:], op=ALU.mult),
                waits=[P.fre(("slabw", 0, tl)),
                       fre(("eacc", tl % 2))], inc=VE)
            P.free[("slabw", 0, tl)] = cv1
            cv2 = op("vector", lambda h, tl=tl: h.tensor_tensor(
                out=outb[tl % 2][:], in0=accst[tl % 2][:], in1=sx[tl % 2][:],
                op=ALU.add),
                waits=[cxl, fre(("outb", tl % 2)), fre(("outbS", tl % 2)),
                       fre(("outbV", tl % 2))], inc=VE)
            P.free[("eacc", tl % 2)] = cv2
            P.free[("sxV", tl % 2)] = cv2
            cv2 = vtok()
            cst = op("sync", lambda h, t=t, tl=tl: h.dma_start(
                out=y_d[t * 128:(t + 1) * 128, :], in_=outb[tl % 2][:]),
                waits=[cv2], inc=LD, n=16)
            P.free[("outb", tl % 2)] = cst
            P.free[("outbS", tl % 2)] = None
            P.free[("outbV", tl % 2)] = None

    return _finish(nc, P, es)


def _finish(nc, P, es):
    fin_ld = P.cnt["sLD"]
    fin_gd = P.cnt["sGD"]
    with es:
        with (nc.semaphore("sLD") as hLD, nc.semaphore("sGD") as hGD,
              nc.semaphore("sPE") as hPE, nc.semaphore("sSC") as hSC,
              nc.semaphore("sVE") as hVE, nc.Block() as block):
            S = {"sLD": hLD, "sGD": hGD, "sPE": hPE, "sSC": hSC, "sVE": hVE}

            def runq(name, h):
                use_drain = name in ("vector", "scalar")
                for ws, make, tgt in P.q[name]:
                    for s, v in ws:
                        h.wait_ge(S[s], v)
                    ins = make(h)
                    if tgt is not None:
                        if use_drain:
                            h.drain(fusable=True).then_inc(S[tgt[0]], tgt[1])
                        else:
                            ins.then_inc(S[tgt[0]], tgt[1])

            @block.sync
            def _(h):
                runq("sync", h)
                h.wait_ge(S["sLD"], fin_ld)

            @block.scalar
            def _(h):
                runq("scalar", h)

            @block.vector
            def _(h):
                runq("vector", h)

            @block.tensor
            def _(h):
                runq("tensor", h)

            @block.gpsimd
            def _(h):
                runq("gpsimd", h)
                h.wait_ge(S["sGD"], fin_gd)
    return nc


def kernel(**inputs):
    maps = _host_maps(inputs)
    nc = _build_nc()
    res = run_bass_kernel_spmd(nc, maps, core_ids=list(range(8)))
    out = np.empty((B, T, C), np.float32)
    for ci, r in enumerate(res.results):
        b, half = ci // 2, ci % 2
        out[b, half * OWN:(half + 1) * OWN] = r["y"]
    return out


if __name__ == "__main__":
    nc = _build_nc()
    print("instructions:", sum(1 for _ in nc.all_instructions()))
